# version 07
# baseline (speedup 1.0000x reference)
import numpy as np
import concourse.bass as bass
import concourse.mybir as mybir
from concourse.bass_utils import run_bass_kernel_spmd

F32 = mybir.dt.float32
BF16 = mybir.dt.bfloat16
AF = mybir.ActivationFunctionType
ALU = mybir.AluOpType
AX = mybir.AxisListType

B, S, D = 8, 8192, 1024
T = 1024
NB = S // T
NTT = T // 128
NDT = D // 128
EX = 512
NET = EX // 128
NEXP = 18
NRE = 16

CB_RB = 0
CB_BL = 16
CB_RW = 48
CB_EPS_LN = 1072
CB_EPS_RMS = 1073
CB_MHALF = 1074
CB_INVD = 1075
CB_RB8 = 1080
CB_BL8 = 1208
CB_W = 1464


class Sem:
    def __init__(self, nc, name, owner, step=1):
        self.name, self.owner, self.step = name, owner, step
        self.h = nc.alloc_semaphore(name)
        self.n = 0
        self.val = {}

    def record(self, ev):
        assert ev not in self.val, f"dup event {self.name}:{ev}"
        self.n += 1
        self.val[ev] = self.n * self.step

    def alias(self, new, old):
        self.val[new] = self.val[old]


class Px:

    def __init__(self, mode, eng=None):
        self.mode = mode
        self.eng = eng
        self.hwm = {}

    def op(self, engine, waits, fn, inc=None, fwaits=()):
        if self.mode is None:
            if inc is not None:
                inc[0].record(inc[1])
            return
        if self.mode != engine:
            return
        e = self.eng
        for sem, ev in waits:
            if sem.owner == engine:
                continue
            tgt = sem.val[ev]
            if self.hwm.get(sem.name, 0) >= tgt:
                continue
            e.wait_ge(sem.h, tgt)
            self.hwm[sem.name] = tgt
        for sem, ev in fwaits:
            tgt = sem.val[ev]
            if self.hwm.get(sem.name, 0) >= tgt:
                continue
            e.wait_ge(sem.h, tgt)
            self.hwm[sem.name] = tgt
        last = fn(e)
        if inc is not None:
            sem, ev = inc
            assert last is not None, f"inc {ev} with no instruction"
            last.then_inc(sem.h, sem.step)

    def alias(self, sem, new, old):
        if self.mode is None:
            sem.alias(new, old)


class NS:
    pass


def build_tensors(nc):
    t = NS()
    dt_ = nc.dram_tensor
    t.x_d = dt_("x", [S, D], F32, kind="ExternalInput").ap()
    t.y_d = dt_("y", [S, D], F32, kind="ExternalOutput").ap()
    t.up_d = dt_("up", [NRE, D, EX], F32, kind="ExternalInput").ap()
    t.gate_d = dt_("gate", [NRE, D, EX], F32, kind="ExternalInput").ap()
    t.down_d = dt_("down", [NRE, EX, D], F32, kind="ExternalInput").ap()
    t.shu_t = dt_("shu_t", [D, D], F32, kind="ExternalInput").ap()
    t.shg_t = dt_("shg_t", [D, D], F32, kind="ExternalInput").ap()
    t.shd_t = dt_("shd_t", [D, D], F32, kind="ExternalInput").ap()
    t.wgate_t = dt_("wgate_t", [D, D], F32, kind="ExternalInput").ap()
    t.router_t = dt_("router_t", [D, 16], F32, kind="ExternalInput").ap()
    t.wlamb_t = dt_("wlamb_t", [D, 32], F32, kind="ExternalInput").ap()
    t.wct_d = dt_("wct", [16, D], F32, kind="ExternalInput").ap()
    t.cbias_d = dt_("cbias", [128, CB_W], F32, kind="ExternalInput").ap()
    t.identb_d = dt_("identb", [128, 128], BF16, kind="ExternalInput").ap()
    t.identf_d = dt_("identf", [128, 128], F32, kind="ExternalInput").ap()
    t.wexp_bf = dt_("wexp_bf", [NEXP, D, 1024], BF16, kind="Internal").ap()
    t.wdn_bf = dt_("wdn_bf", [NEXP, EX, D], BF16, kind="Internal").ap()
    t.wgate_bf = dt_("wgate_bf", [D, D], BF16, kind="Internal").ap()

    sb = nc.alloc_sbuf_tensor
    t.xbuf = [sb(f"xbuf{i}", [128, NTT * D], F32) for i in range(2)]
    t.xlnT = sb("xlnT", [128, NDT * T], BF16)
    t.hidden = [sb(f"hidden{i}", [128, NET * T], BF16) for i in range(2)]
    t.wug = [sb(f"wug{i}", [128, NDT * 1024], BF16) for i in range(2)]
    t.wdn = [sb(f"wdn{i}", [128, NET * 1024], BF16) for i in range(2)]
    t.sg = [sb(f"sg{i}", [128, 512], F32) for i in range(2)]
    t.gbuf = [sb("gbuf0", [128, 1024], F32)]
    t.ytile = [sb(f"ytile{i}", [128, 1024], F32) for i in range(4)]
    t.tsb = sb("tsb", [128, 1024], F32)
    t.dump = sb("dump", [128, 1024], F32)
    t.xln_tile = [sb(f"xlnt{i}", [128, 1024], BF16) for i in range(2)]
    t.cb = sb("cb", [128, CB_W], F32)
    t.identb = sb("identb_s", [128, 128], BF16)
    t.identf = sb("identf_s", [128, 128], F32)
    t.rw_s = sb("rw_s", [128, NDT * 16], BF16)
    t.lb_s = sb("lb_s", [128, NDT * 32], BF16)
    t.wct_s = sb("wct_s", [16, D], BF16)
    t.zlamT = sb("zlamT", [16, T], F32)
    t.zbxT = sb("zbxT", [16, T], F32)
    t.hs = sb("hs", [16, T], F32)
    t.hs_bf = sb("hs_bf", [16, T], BF16)
    t.hcarry = sb("hcarry", [16, 1], F32)
    t.zrt = sb("zrt", [128, NTT * 16], F32)
    t.top8 = sb("top8", [128, NTT * 8], F32)
    t.rsc = sb("rsc", [128, NTT * 4], F32)
    t.mask = sb("mask", [128, NTT * 16], F32)
    t.ch1 = sb("ch1", [128, NTT * 16], F32)
    t.chalf = [sb(f"chalf{i}", [128, NTT * 16], F32) for i in range(2)]
    t.stats = sb("stats", [128, NTT * 8], F32)
    t.sstat = sb("sstat", [128, NTT * 2], F32)
    t.zlb = sb("zlb", [128, NTT * 32], F32)

    pt = nc.alloc_psum_tensor
    t.ps = [pt(f"ps{i}", [128, 1024], F32) for i in range(3)]
    t.psTR = pt("psTR", [128, 1024], BF16)
    t.psTR2 = pt("psTR2", [128, 1024], BF16)

    t.s_x = Sem(nc, "s_x", "dma_sp", 16)
    t.s_cst = Sem(nc, "s_cst", "dma_sp", 16)
    t.s_wa = [Sem(nc, f"s_wa{j}", "dma_sp", 16) for j in range(4)]
    t.s_wd = [Sem(nc, f"s_wd{j}", "dma_sp", 16) for j in range(4)]
    t.s_st = Sem(nc, "s_st", "dma_sp", 16)
    t.s_cv = [Sem(nc, f"s_cv{k}", "dma_gp", 16) for k in range(NEXP)]
    t.s_cvw = Sem(nc, "s_cvw", "dma_gp", 16)
    t.s_gw = Sem(nc, "s_gw", "dma_gp", 16)
    t.s_pe = Sem(nc, "s_pe", "pe", 1)
    t.s_dve = Sem(nc, "s_dve", "dve", 1)
    t.s_act = Sem(nc, "s_act", "act", 1)
    return t


def conv_srcs(t, k):
    if k < NRE:
        return t.up_d[k], t.gate_d[k], t.down_d[k]
    j = k - NRE
    return (t.shu_t[:, j * 512:(j + 1) * 512],
            t.shg_t[:, j * 512:(j + 1) * 512],
            t.shd_t[j * 512:(j + 1) * 512, :])


def emit(P, t):
    s_x, s_st, s_cst = t.s_x, t.s_st, t.s_cst
    s_wa, s_wd = t.s_wa, t.s_wd
    s_cv, s_cvw, s_gw = t.s_cv, t.s_cvw, t.s_gw
    s_pe, s_dve, s_act = t.s_pe, t.s_dve, t.s_act

    P.op("sp", [], lambda e: e.dma_start(out=t.cb[:], in_=t.cbias_d), (s_cst, "cb"))
    P.op("sp", [], lambda e: e.dma_start(out=t.identb[:], in_=t.identb_d), (s_cst, "idb"))
    P.op("sp", [], lambda e: e.dma_start(out=t.identf[:], in_=t.identf_d), (s_cst, "idf"))
    P.op("gp", [], lambda e: e.dma_start(
        out=t.rw_s.ap().rearrange("p (dt n) -> p dt n", n=16),
        in_=t.router_t.rearrange("(dt p) n -> p dt n", p=128)), (s_gw, "rw"))
    P.op("gp", [], lambda e: e.dma_start(
        out=t.lb_s.ap().rearrange("p (dt n) -> p dt n", n=32),
        in_=t.wlamb_t.rearrange("(dt p) n -> p dt n", p=128)), (s_gw, "lb"))
    P.op("gp", [], lambda e: e.dma_start(out=t.wct_s[:], in_=t.wct_d), (s_gw, "wct"))
    for k in range(NEXP):
        us, gs, ds = conv_srcs(t, k)
        P.op("gp", [], lambda e, k=k, us=us: e.dma_start(
            out=t.wexp_bf[k, :, 0:512], in_=us), (s_cv[k], f"cvu{k}"))
        P.op("gp", [], lambda e, k=k, gs=gs: e.dma_start(
            out=t.wexp_bf[k, :, 512:1024], in_=gs), (s_cv[k], f"cvg{k}"))
        P.op("gp", [], lambda e, k=k, ds=ds: e.dma_start(
            out=t.wdn_bf[k], in_=ds), (s_cv[k], f"cvd{k}"))
    P.op("gp", [], lambda e: e.dma_start(out=t.wgate_bf, in_=t.wgate_t), (s_cvw, "cwg"))
    P.op("dve", [], lambda e: e.memset(t.hcarry[:], 0.0), (s_dve, "hc0"))

    st = NS()
    st.cc = 0
    st.pair_free = {}
    st.nslot = 0
    st.hid_guard = {}
    st.wug_ev = {}
    st.wdn_ev = {}

    def pair_waits(cc):
        w = []
        if cc - 3 in st.pair_free:
            w.append(st.pair_free[cc - 3])
        return w

    def xload(b):
        xp = b % 2
        waits = [(s_st, f"y{b-2}")] if b >= 2 else []
        P.op("sp", waits, lambda e, b=b, xp=xp: e.dma_start(
            out=t.xbuf[xp].ap().rearrange("p (tt d) -> p tt d", d=D),
            in_=t.x_d[b * T:(b + 1) * T, :].rearrange("(tt p) d -> p tt d", p=128)),
            (s_x, f"x{b}"))

    def load_slot(kind, b, p=None, k=None):
        i = st.nslot
        st.nslot += 1
        par = i % 2
        waits = []
        if i >= 2:
            waits.append((s_pe, f"slotdone{i-2}"))
        sa, sd = s_wa[i % 4], s_wd[i % 4]
        if kind == "e":
            uw = [(s_cv[k], f"cvd{k}")]
            if i >= 2:
                uw.append((s_pe, f"ugdone{i-2}"))
            P.op("sp", uw, lambda e, k=k, par=par: e.dma_start(
                out=t.wug[par].ap().rearrange("p (dt c) -> p dt c", c=1024),
                in_=t.wexp_bf[k].rearrange("(dt p) c -> p dt c", p=128)),
                (sa, f"wa{i}"))
            P.op("sp", waits + [(s_cv[k], f"cvd{k}")],
                 lambda e, k=k, par=par: e.dma_start(
                     out=t.wdn[par].ap().rearrange("p (et c) -> p et c", c=1024),
                     in_=t.wdn_bf[k].rearrange("(et p) c -> p et c", p=128)),
                 (sd, f"w{i}"))
            st.wdn_ev[i] = (sd, f"w{i}")
        else:
            waits = [(s_cvw, "cwg")]
            if i >= 2:
                waits.append((s_pe, f"ugdone{i-2}"))
            P.op("sp", waits, lambda e, par=par: e.dma_start(
                out=t.wug[par].ap().rearrange("p (dt c) -> p dt c", c=1024),
                in_=t.wgate_bf.rearrange("(dt p) c -> p dt c", p=128)),
                (sa, f"wa{i}"))
        st.wug_ev[i] = (sa, f"wa{i}")
        return i, par

    def _xw_of(b, ph, tt):
        if ph == 0:
            return [(s_x, f"x{b}")]
        if ph == 1:
            return [(s_dve, f"dnd_b{b}p0k{NEXP-1}tt{tt}")]
        return [(s_dve, f"sadd{b}tt{tt}")]

    def _sc(tt):
        return t.stats[:, tt * 8:tt * 8 + 8]

    def ln_s1(b, ph, xp, tt):
        pre = f"{b}p{ph}"
        xw = _xw_of(b, ph, tt)
        x_ap = t.xbuf[xp][:, tt * D:(tt + 1) * D]
        P.op("dve", xw, lambda e: e.reduce_sum(
            _sc(tt)[:, 0:1], x_ap, axis=AX.X), (s_dve, f"lns{pre}t{tt}"))
        P.op("act", xw, lambda e: e.activation(
            t.dump[:], x_ap, AF.Square, accum_out=_sc(tt)[:, 1:2]),
            (s_act, f"lnq{pre}t{tt}"))
        P.op("act", [(s_dve, f"lns{pre}t{tt}"), (s_cst, "idf")],
             lambda e: e.activation(
                 _sc(tt)[:, 6:7], _sc(tt)[:, 0:1], AF.Square,
                 scale=t.cb[:, CB_INVD:CB_INVD + 1]),
             (s_act, f"lnq2{pre}t{tt}"))

    def ln_s24(b, ph):
        pre = f"{b}p{ph}"
        for tt in range(NTT):
            P.op("dve", [(s_act, f"lnq{pre}t{tt}"), (s_act, f"lnq2{pre}t{tt}")],
                 lambda e, tt=tt: e.scalar_tensor_tensor(
                     _sc(tt)[:, 3:4], _sc(tt)[:, 1:2], 1.0 / D,
                     _sc(tt)[:, 6:7], ALU.mult, ALU.subtract),
                 (s_dve, f"lnv{pre}t{tt}"))
        for tt in range(NTT):
            P.op("act", [(s_dve, f"lnv{pre}t{tt}"), (s_cst, "idf")],
                 lambda e, tt=tt: e.activation(
                     _sc(tt)[:, 4:5], _sc(tt)[:, 3:4], AF.Ln,
                     bias=t.cb[:, CB_EPS_LN:CB_EPS_LN + 1]),
                 (s_act, f"lnr1{pre}t{tt}"))
        for tt in range(NTT):
            P.op("act", [], lambda e, tt=tt: e.activation(
                _sc(tt)[:, 4:5], _sc(tt)[:, 4:5], AF.Exp,
                scale=t.cb[:, CB_MHALF:CB_MHALF + 1]),
                (s_act, f"lnr{pre}t{tt}"), fwaits=[(s_act, f"lnr1{pre}t{tt}")])
        for tt in range(NTT):
            P.op("dve", [(s_act, f"lnr{pre}t{tt}")],
                 lambda e, tt=tt: e.scalar_tensor_tensor(
                     _sc(tt)[:, 5:6], _sc(tt)[:, 0:1], -1.0 / D,
                     _sc(tt)[:, 4:5], ALU.mult, ALU.mult),
                 (s_dve, f"lnm{pre}t{tt}"))

    def ln_rest(b, ph, xp):
        pre = f"{b}p{ph}"

        def x_ap(tt):
            return t.xbuf[xp][:, tt * D:(tt + 1) * D]

        def norm_op(tt):
            nw = [(s_dve, f"lnm{pre}t{tt}")]
            if tt >= 2:
                nw.append((s_pe, f"tr{pre}t{tt-2}"))
            P.op("act", nw, lambda e, tt=tt: e.activation(
                t.xln_tile[tt % 2][:], x_ap(tt), AF.Identity,
                bias=_sc(tt)[:, 5:6], scale=_sc(tt)[:, 4:5]),
                (s_act, f"lnn{pre}t{tt}"), fwaits=[(s_act, f"lnr{pre}t{tt}")])

        def trcopy_op(tt):
            ptr = t.psTR if tt % 2 == 0 else t.psTR2
            P.op("act", [(s_pe, f"tr{pre}t{tt}")],
                 lambda e, tt=tt, ptr=ptr: e.copy(
                     t.xlnT.ap().rearrange("p (dt c) -> p dt c", c=T)
                     [:, :, tt * 128:(tt + 1) * 128],
                     ptr.ap().rearrange("p (dt c) -> p dt c", c=128)),
                 (s_act, f"trc{pre}t{tt}"))

        for tt in range(NTT):
            norm_op(tt)
            if tt >= 1:
                trcopy_op(tt - 1)
        trcopy_op(NTT - 1)
        for tt in range(NTT):
            ptr = t.psTR if tt % 2 == 0 else t.psTR2
            trw = [(s_act, f"lnn{pre}t{tt}"), (s_cst, "idf")]
            if tt >= 2:
                trw.append((s_act, f"trc{pre}t{tt-2}"))

            def trops(e, tt=tt, ptr=ptr):
                last = None
                for dt in range(NDT):
                    last = e.transpose(ptr[:, dt * 128:(dt + 1) * 128],
                                       t.xln_tile[tt % 2][:, dt * 128:(dt + 1) * 128],
                                       t.identb[:])
                return last
            P.op("pe", trw, trops, (s_pe, f"tr{pre}t{tt}"))

        if ph != 1:
            router(b, ph)


    def router(b, ph):
        pre = f"{b}p{ph}"
        cc = st.cc
        st.cc += 1
        ps = t.ps[cc % 3]
        for tt in range(NTT):
            w = [(s_act, f"trc{pre}t{tt}"), (s_gw, "wct")]
            if tt == 0:
                w += pair_waits(cc)

            def rops(e, tt=tt, ps=ps):
                last = None
                for dt in range(NDT):
                    last = e.matmul(ps[:, tt * 16:(tt + 1) * 16],
                                    t.xlnT[:, dt * T + tt * 128: dt * T + (tt + 1) * 128],
                                    t.rw_s[:, dt * 16:(dt + 1) * 16],
                                    start=(dt == 0), stop=(dt == NDT - 1))
                return last
            P.op("pe", w, rops,
                 (s_pe, f"pf{cc}") if tt == NTT - 1 else None)
        P.op("dve", [(s_pe, f"pf{cc}"), (s_cst, "idf")],
             lambda e, ps=ps: e.tensor_add(t.zrt[:], ps[:, 0:NTT * 16],
                                           t.cb[:, CB_RB8:CB_RB8 + NTT * 16]),
             (s_dve, f"pd{cc}"))
        st.pair_free[cc] = (s_dve, f"pd{cc}")

        def sl(buf, tt, wdt):
            return buf[:, tt * wdt:(tt + 1) * wdt]

        for tt in range(NTT):
            P.op("dve", [], lambda e, tt=tt: e.max(sl(t.top8, tt, 8),
                                                   sl(t.zrt, tt, 16)),
                 (s_dve, f"rmx{pre}t{tt}"), fwaits=[(s_dve, f"pd{cc}")])
        for tt in range(NTT):
            P.op("dve", [], lambda e, tt=tt: e.tensor_sub(
                sl(t.rsc, tt, 4)[:, 0:1], sl(t.top8, tt, 8)[:, 0:1],
                sl(t.top8, tt, 8)[:, 1:2]),
                (s_dve, f"d12{pre}t{tt}"), fwaits=[(s_dve, f"rmx{pre}t{tt}")])
        for tt in range(NTT):
            P.op("act", [(s_dve, f"d12{pre}t{tt}")],
                 lambda e, tt=tt: e.activation(sl(t.rsc, tt, 4)[:, 1:2],
                                               sl(t.rsc, tt, 4)[:, 0:1],
                                               AF.Sigmoid),
                 (s_act, f"s1{pre}t{tt}"))
        for tt in range(NTT):
            def combA(e, tt=tt):
                rs = sl(t.rsc, tt, 4)
                e.tensor_scalar_mul(rs[:, 2:3], rs[:, 1:2], 0.5)
                e.tensor_scalar(rs[:, 3:4], rs[:, 1:2], -0.5, 0.5,
                                ALU.mult, ALU.add)
                return e.tensor_scalar(sl(t.mask, tt, 16), sl(t.zrt, tt, 16),
                                       sl(t.top8, tt, 8)[:, 0:1], None,
                                       ALU.is_equal)
            P.op("dve", [(s_act, f"s1{pre}t{tt}")], combA, (s_dve, f"rmA{pre}t{tt}"))
        for tt in range(NTT):
            def combB(e, tt=tt):
                rs = sl(t.rsc, tt, 4)
                e.tensor_scalar(sl(t.ch1, tt, 16), sl(t.mask, tt, 16),
                                rs[:, 2:3], None, ALU.mult)
                return e.tensor_scalar(sl(t.mask, tt, 16), sl(t.zrt, tt, 16),
                                       sl(t.top8, tt, 8)[:, 1:2], None,
                                       ALU.is_equal)
            P.op("dve", [], combB, (s_dve, f"rmB{pre}t{tt}"),
                 fwaits=[(s_dve, f"rmA{pre}t{tt}")])
        chb = t.chalf[0 if ph == 0 else 1]
        for tt in range(NTT):
            P.op("dve", [], lambda e, tt=tt: e.scalar_tensor_tensor(
                sl(chb, tt, 16), sl(t.mask, tt, 16),
                sl(t.rsc, tt, 4)[:, 3:4], sl(t.ch1, tt, 16),
                ALU.mult, ALU.add),
                (s_dve, f"rmC{pre}t{tt}"), fwaits=[(s_dve, f"rmB{pre}t{tt}")])

    def ffn_pass(b, p, xp, tail_s1=None, tail_s24=None):
        lnph = 0 if p == 0 else 2
        slots = []
        for k in range(NEXP):
            slots.append(load_slot("e", b, p, k))

        def fills(k, i, par):
            hb = t.hidden[k % 2]
            lastug = None
            for et in range(NET):
                for tc in range(T // 512):
                    cc = st.cc
                    st.cc += 1
                    ps = t.ps[cc % 3]
                    w = [st.wug_ev[i],
                         (s_act, f"trc{b}p{lnph}t{tc * 4 + 3}")]
                    w += pair_waits(cc)
                    sgp = cc % 2

                    def fillug(e, par=par, et=et, tc=tc, ps=ps):
                        last = None
                        for half, base in ((0, 0), (1, 512)):
                            for dt in range(NDT):
                                wcol = dt * 1024 + half * 512 + et * 128
                                last = e.matmul(
                                    ps[:, half * 512:half * 512 + 512],
                                    t.wug[par][:, wcol:wcol + 128],
                                    t.xlnT[:, dt * T + tc * 512: dt * T + (tc + 1) * 512],
                                    start=(dt == 0), stop=(dt == NDT - 1))
                        return last
                    P.op("pe", w, fillug, (s_pe, f"pf{cc}"))
                    sw = [(s_pe, f"pf{cc}")]
                    sgk = f"sgfree{sgp}"
                    if getattr(st, sgk, None) is not None:
                        sw.append(getattr(st, sgk))
                    P.op("act", sw, lambda e, ps=ps, sgp=sgp: e.activation(
                        t.sg[sgp][:], ps[:, 512:1024], AF.Silu),
                        (s_act, f"sil{cc}"))
                    hcol = et * T + tc * 512
                    dw = [(s_pe, f"pf{cc}"), (s_act, f"sil{cc}")]
                    hg = st.hid_guard.get(k % 2)
                    if hg is not None:
                        dw.append(hg)
                    P.op("dve", dw,
                         lambda e, ps=ps, sgp=sgp, hcol=hcol, hb=hb: e.tensor_mul(
                             hb[:, hcol:hcol + 512], ps[:, 0:512],
                             t.sg[sgp][:]), (s_dve, f"pd{cc}"))
                    st.pair_free[cc] = (s_dve, f"pd{cc}")
                    setattr(st, sgk, (s_dve, f"pd{cc}"))
                    lastug = cc
            P.alias(s_pe, f"ugdone{i}", f"pf{lastug}")
            return (s_dve, f"pd{lastug}")

        def downs(k, i, par, hid_done):
            hb = t.hidden[k % 2]
            for tt in range(NTT):
                cc = st.cc
                st.cc += 1
                ps = t.ps[cc % 3]
                w = [hid_done, st.wdn_ev[i]] + pair_waits(cc)

                def filldn(e, par=par, tt=tt, ps=ps, hb=hb):
                    last = None
                    for fc in range(2):
                        for et in range(NET):
                            last = e.matmul(
                                ps[:, fc * 512:(fc + 1) * 512],
                                hb[:, et * T + tt * 128: et * T + (tt + 1) * 128],
                                t.wdn[par][:, et * 1024 + fc * 512: et * 1024 + (fc + 1) * 512],
                                start=(et == 0), stop=(et == NET - 1))
                    return last
                P.op("pe", w, filldn, (s_pe, f"pf{cc}"))
                if tt == NTT - 1:
                    P.alias(s_pe, f"slotdone{i}", f"pf{cc}")
                    st.hid_guard[k % 2] = (s_pe, f"pf{cc}")
                xt = t.xbuf[xp][:, tt * D:(tt + 1) * D]
                scal = (t.chalf[p][:, tt * 16 + k: tt * 16 + k + 1]
                        if k < NRE else 0.5)
                P.op("dve", [(s_pe, f"pf{cc}")],
                     lambda e, ps=ps, xt=xt, scal=scal: e.scalar_tensor_tensor(
                         xt, ps[:], scal, xt, ALU.mult, ALU.add),
                     (s_dve, f"pd{cc}"))
                st.pair_free[cc] = (s_dve, f"pd{cc}")
                P.alias(s_dve, f"dnd_b{b}p{p}k{k}tt{tt}", f"pd{cc}")
                if k == NEXP - 1 and tail_s1 is not None:
                    tail_s1(tt)

        prev = None
        for k in range(NEXP):
            i, par = slots[k]
            hid_done = fills(k, i, par)
            if prev is not None:
                downs(*prev)
            prev = (k, i, par, hid_done)
        downs(*prev)
        if tail_s24 is not None:
            tail_s24()

    def ssm_phase(b, xp):
        ln_rest(b, 1, xp)
        iwg, pwg = load_slot("wg", b)
        ccZ = st.cc
        st.cc += 1
        psZ = t.ps[ccZ % 3]
        for tt in range(NTT):
            w = [(s_act, f"trc{b}p1t{tt}"), (s_gw, "wct")]
            if tt == 0:
                w += pair_waits(ccZ)

            def lbops(e, tt=tt, psZ=psZ):
                last = None
                for dt in range(NDT):
                    last = e.matmul(psZ[:, tt * 32:(tt + 1) * 32],
                                    t.xlnT[:, dt * T + tt * 128: dt * T + (tt + 1) * 128],
                                    t.lb_s[:, dt * 32:(dt + 1) * 32],
                                    start=(dt == 0), stop=(dt == NDT - 1))
                return last
            P.op("pe", w, lbops, (s_pe, f"pf{ccZ}") if tt == NTT - 1 else None)
        P.op("dve", [(s_pe, f"pf{ccZ}"), (s_cst, "idf")],
             lambda e, psZ=psZ: e.tensor_add(
                 t.zlb[:], psZ[:, 0:NTT * 32],
                 t.cb[:, CB_BL8:CB_BL8 + NTT * 32]),
             (s_dve, f"pd{ccZ}"))
        st.pair_free[ccZ] = (s_dve, f"pd{ccZ}")
        ccL = st.cc
        st.cc += 1
        psL = t.ps[ccL % 3]

        def latr(e, psL=psL):
            last = None
            for tt in range(NTT):
                last = e.transpose(psL[0:16, tt * 128:(tt + 1) * 128],
                                   t.zlb[:, tt * 32:tt * 32 + 16], t.identf[:])
            return last
        P.op("pe", [(s_dve, f"pd{ccZ}"), (s_cst, "idf")] + pair_waits(ccL),
             latr, (s_pe, f"pf{ccL}"))
        P.op("act", [(s_pe, f"pf{ccL}")],
             lambda e, psL=psL: e.copy(t.zlamT[:], psL[0:16, 0:T]),
             (s_act, f"pd{ccL}"))
        st.pair_free[ccL] = (s_act, f"pd{ccL}")
        ccB = st.cc
        st.cc += 1
        psB = t.ps[ccB % 3]

        def bxtr(e, psB=psB):
            last = None
            for tt in range(NTT):
                last = e.transpose(psB[0:16, tt * 128:(tt + 1) * 128],
                                   t.zlb[:, tt * 32 + 16:tt * 32 + 32], t.identf[:])
            return last
        P.op("pe", pair_waits(ccB), bxtr, (s_pe, f"pf{ccB}"))
        P.op("act", [(s_pe, f"pf{ccB}")],
             lambda e, psB=psB: e.copy(t.zbxT[:], psB[0:16, 0:T]),
             (s_act, f"pd{ccB}"))
        st.pair_free[ccB] = (s_act, f"pd{ccB}")
        P.op("act", [], lambda e: e.activation(t.zlamT[:], t.zlamT[:], AF.Sigmoid),
             (s_act, f"sig{b}"), fwaits=[(s_act, f"pd{ccL}")])
        P.op("dve", [(s_act, f"sig{b}"), (s_act, f"pd{ccB}")],
             lambda e: e.tensor_tensor_scan(t.hs[:], t.zlamT[:], t.zbxT[:],
                                            t.hcarry[:, 0:1], ALU.mult, ALU.add),
             (s_dve, f"scan{b}"))

        def hscopy(e):
            e.tensor_copy(t.hcarry[:, 0:1], t.hs[:, T - 1:T])
            return e.tensor_copy(t.hs_bf[:], t.hs[:])
        P.op("dve", [], hscopy, (s_dve, f"hsbf{b}"), fwaits=[(s_dve, f"scan{b}")])

        for g in range(NTT // 4):
            tts = range(g * 4, g * 4 + 4)
            for tt in tts:
                ccg = st.cc
                st.cc += 1
                psg = t.ps[ccg % 3]
                w = [st.wug_ev[iwg], (s_act, f"trc{b}p1t{tt}")] + pair_waits(ccg)

                def gops(e, tt=tt, psg=psg, pwg=pwg):
                    last = None
                    for fc in range(2):
                        for dt in range(NDT):
                            last = e.matmul(
                                psg[:, fc * 512:(fc + 1) * 512],
                                t.xlnT[:, dt * T + tt * 128: dt * T + (tt + 1) * 128],
                                t.wug[pwg][:, dt * 1024 + fc * 512: dt * 1024 + (fc + 1) * 512],
                                start=(dt == 0), stop=(dt == NDT - 1))
                    return last
                P.op("pe", w, gops, (s_pe, f"pf{ccg}"))
                if tt == NTT - 1:
                    P.alias(s_pe, f"slotdone{iwg}", f"pf{ccg}")
                    P.alias(s_pe, f"ugdone{iwg}", f"pf{ccg}")
                ccy = st.cc
                st.cc += 1
                psy = t.ps[ccy % 3]
                w = [(s_dve, f"hsbf{b}"), (s_gw, "wct")] + pair_waits(ccy)

                def yops(e, tt=tt, psy=psy):
                    last = None
                    for fc in range(2):
                        last = e.matmul(psy[:, fc * 512:(fc + 1) * 512],
                                        t.hs_bf[:, tt * 128:(tt + 1) * 128],
                                        t.wct_s[:, fc * 512:(fc + 1) * 512],
                                        start=True, stop=True)
                    return last
                P.op("pe", w, yops, (s_pe, f"pf{ccy}"))
                gp = 0
                yp = tt % 4
                gw_ = [(s_pe, f"pf{ccg}")]
                if tt >= 1:
                    gw_.append((s_dve, f"ymul{b}t{tt-1}"))
                P.op("act", gw_, lambda e, psg=psg, gp=gp: e.activation(
                    t.gbuf[gp][:], psg[:], AF.Sigmoid), (s_act, f"gs{b}t{tt}"))
                st.pair_free[ccg] = (s_act, f"gs{b}t{tt}")
                yw = [(s_pe, f"pf{ccy}"), (s_act, f"gs{b}t{tt}")]
                if tt >= 4:
                    yw.append((s_act, f"sq2{b}t{tt-4}"))
                P.op("dve", yw, lambda e, psy=psy, gp=gp, yp=yp: e.tensor_mul(
                    t.ytile[yp][:], psy[:], t.gbuf[gp][:]), (s_dve, f"ymul{b}t{tt}"))
                st.pair_free[ccy] = (s_dve, f"ymul{b}t{tt}")
                P.op("act", [(s_dve, f"ymul{b}t{tt}")],
                     lambda e, yp=yp, tt=tt: e.activation(
                         t.dump[:], t.ytile[yp][:], AF.Square,
                         accum_out=t.sstat[:, tt * 2:tt * 2 + 1]),
                     (s_act, f"sq2{b}t{tt}"))
            for tt in tts:
                P.op("act", [(s_cst, "idf")], lambda e, tt=tt: e.activation(
                    t.sstat[:, tt * 2 + 1:tt * 2 + 2],
                    t.sstat[:, tt * 2:tt * 2 + 1], AF.Ln,
                    bias=t.cb[:, CB_EPS_RMS:CB_EPS_RMS + 1],
                    scale=t.cb[:, CB_INVD:CB_INVD + 1]),
                    (s_act, f"r2a{b}t{tt}"), fwaits=[(s_act, f"sq2{b}t{tt}")])
            for tt in tts:
                P.op("act", [], lambda e, tt=tt: e.activation(
                    t.sstat[:, tt * 2 + 1:tt * 2 + 2],
                    t.sstat[:, tt * 2 + 1:tt * 2 + 2], AF.Exp,
                    scale=t.cb[:, CB_MHALF:CB_MHALF + 1]),
                    (s_act, f"rs2{b}t{tt}"), fwaits=[(s_act, f"r2a{b}t{tt}")])
            for tt in tts:
                def saddops(e, tt=tt):
                    yp = tt % 4
                    ss = t.sstat[:, tt * 2:tt * 2 + 2]
                    xt = t.xbuf[xp][:, tt * D:(tt + 1) * D]
                    e.scalar_tensor_tensor(t.tsb[:], t.ytile[yp][:], ss[:, 1:2],
                                           t.cb[:, CB_RW:CB_RW + D],
                                           ALU.mult, ALU.mult)
                    return e.tensor_add(xt, xt, t.tsb[:])
                P.op("dve", [(s_act, f"rs2{b}t{tt}"), (s_cst, "idf")], saddops,
                     (s_dve, f"sadd{b}tt{tt}"))
                ln_s1(b, 2, xp, tt)
        ln_s24(b, 2)

    xload(0)
    for b in range(NB):
        xp = b % 2
        if b == 0:
            for tt in range(NTT):
                ln_s1(0, 0, 0, tt)
            ln_s24(0, 0)
        ln_rest(b, 0, xp)
        ffn_pass(b, 0, xp,
                 tail_s1=lambda tt, b=b, xp=xp: ln_s1(b, 1, xp, tt),
                 tail_s24=lambda b=b: ln_s24(b, 1))
        ssm_phase(b, xp)
        ln_rest(b, 2, xp)
        if b + 1 < NB:
            xload(b + 1)
            ffn_pass(b, 1, xp,
                     tail_s1=lambda tt, b=b: ln_s1(b + 1, 0, (b + 1) % 2, tt),
                     tail_s24=lambda b=b: ln_s24(b + 1, 0))
        else:
            ffn_pass(b, 1, xp)
        P.op("sp", [(s_dve, f"dnd_b{b}p1k{NEXP-1}tt{NTT-1}")],
             lambda e, b=b, xp=xp: e.dma_start(
                 out=t.y_d[b * T:(b + 1) * T, :].rearrange("(tt p) d -> p tt d", p=128),
                 in_=t.xbuf[xp].ap().rearrange("p (tt d) -> p tt d", d=D)),
             (s_st, f"y{b}"))
    P.op("sp", [(s_st, f"y{NB-1}")], lambda e: e.nop(), None)


_NC_CACHE = {}


def build_program():
    nc = bass.Bass("TRN2", target_bir_lowering=False, debug=False,
                   detect_race_conditions=False)
    t = build_tensors(nc)
    emit(Px(None), t)
    with nc.Block() as blk:
        @blk.sync
        def _(eng):
            emit(Px("sp", eng), t)

        @blk.gpsimd
        def _(eng):
            emit(Px("gp", eng), t)

        @blk.tensor
        def _(eng):
            emit(Px("pe", eng), t)

        @blk.vector
        def _(eng):
            emit(Px("dve", eng), t)

        @blk.scalar
        def _(eng):
            emit(Px("act", eng), t)
    return nc


def build_cbias(router_bias, b_lambda, rms_weight):
    f32 = np.float32
    cb = np.zeros((128, CB_W), dtype=f32)
    cb[:, CB_RB:CB_RB + 16] = np.asarray(router_bias, f32)[None, :]
    cb[:, CB_BL:CB_BL + 16] = np.asarray(b_lambda, f32)[None, :]
    cb[:, CB_RW:CB_RW + D] = np.asarray(rms_weight, f32)[None, :]
    cb[:, CB_EPS_LN] = 1e-5
    cb[:, CB_EPS_RMS] = 1e-6
    cb[:, CB_MHALF] = -0.5
    cb[:, CB_INVD] = 1.0 / D
    rb8 = np.tile(np.asarray(router_bias, f32), NTT)
    cb[:, CB_RB8:CB_RB8 + NTT * 16] = rb8[None, :]
    bl = np.concatenate([np.asarray(b_lambda, f32), np.zeros(16, f32)])
    cb[:, CB_BL8:CB_BL8 + NTT * 32] = np.tile(bl, NTT)[None, :]
    return cb


def kernel(L0_attn_x, L0_attn_W_lambda, L0_attn_b_lambda, L0_attn_W_B,
           L0_attn_W_C, L0_attn_W_gate, L0_attn_rms_weight,
           L0_ffn_router_w, L0_ffn_router_bias, L0_ffn_up_proj,
           L0_ffn_gate_proj, L0_ffn_down_proj, L0_ffn_sh_gate,
           L0_ffn_sh_up, L0_ffn_sh_down):
    import ml_dtypes
    f32 = np.float32
    x = np.ascontiguousarray(np.asarray(L0_attn_x, dtype=f32))
    cb = build_cbias(L0_ffn_router_bias, L0_attn_b_lambda, L0_attn_rms_weight)

    common = {
        "up": np.ascontiguousarray(np.asarray(L0_ffn_up_proj, f32)),
        "gate": np.ascontiguousarray(np.asarray(L0_ffn_gate_proj, f32)),
        "down": np.ascontiguousarray(np.asarray(L0_ffn_down_proj, f32)),
        "shu_t": np.ascontiguousarray(np.asarray(L0_ffn_sh_up, f32).T),
        "shg_t": np.ascontiguousarray(np.asarray(L0_ffn_sh_gate, f32).T),
        "shd_t": np.ascontiguousarray(np.asarray(L0_ffn_sh_down, f32).T),
        "wgate_t": np.ascontiguousarray(np.asarray(L0_attn_W_gate, f32).T),
        "router_t": np.ascontiguousarray(np.asarray(L0_ffn_router_w, f32).T),
        "wlamb_t": np.ascontiguousarray(np.concatenate(
            [np.asarray(L0_attn_W_lambda, f32).T,
             np.asarray(L0_attn_W_B, f32).T], axis=1)),
        "wct": np.ascontiguousarray(np.asarray(L0_attn_W_C, f32).T),
        "cbias": cb,
        "identb": np.eye(128, dtype=ml_dtypes.bfloat16),
        "identf": np.eye(128, dtype=f32),
    }
    in_maps = [dict(common, x=np.ascontiguousarray(x[c])) for c in range(8)]
    nc = build_program()
    res = run_bass_kernel_spmd(nc, in_maps, core_ids=list(range(8)))
    out = np.stack([np.asarray(res.results[c]["y"]) for c in range(8)], axis=0)
    return out.astype(np.float32, copy=False)
